# version 30
# baseline (speedup 1.0000x reference)
import numpy as np
from contextlib import ExitStack

import concourse.bass as bass
import concourse.mybir as mybir
from concourse.bass_utils import run_bass_kernel_spmd

F32 = mybir.dt.float32
BF16 = mybir.dt.bfloat16
AF = mybir.ActivationFunctionType
ALU = mybir.AluOpType
AX = mybir.AxisListType

D = 1024
NST = 16
KCV = 4
E = 8
NK = D // 128
TC = 512
NCH = 8
NCORES = 8
SCP = TC

N_PIECE = 128
RING = 8
PULL = 4
NPULL = N_PIECE // PULL

SI_ORDER = [0, 8, 1, 9, 2, 10, 3, 11, 4, 12, 5, 13, 6, 14, 7, 15]

C_SVST = 8
C_SAST = 6
C_STXP = 32
C_SAXE = 32
C_SVLN = 32
C_SAG = 12
C_STBP = 5
C_SABP = 5
C_SACV = 16
C_SVCV = 17
C_SPCV = 8
C_STGT = 8
C_SAGT = 8
C_SVGT = 48
C_STBC = 128
C_SVBC = 128
C_SADA = 128
C_SVSC = 128
C_SPY = 136
C_SAF = 16
C_SAB = 16
C_SPF = 16
C_SVF = 4
C_STGB = 8
C_SAGB = 8
C_SVEM = 32
C_SVHG = 4
C_STOP = 32
C_SVOC = 32
C_SVOA = 32

T_GAPS = []
for _jj in range(16):
    for _h in range(2):
        T_GAPS.append((2 + 2 * _jj + _h, ("si", _jj, _h)))
for _i in range(4):
    T_GAPS.append((4 + 2 * _i, ("backT", _i)))
for _e in range(8):
    T_GAPS.append((14 + _e, ("gatebc", _e)))
_SO_GAPS = {0: 34, 1: 36, 2: 40, 3: 68}
for _i in range(4):
    for _h in range(2):
        T_GAPS.append((_SO_GAPS[_i] + _h, ("so", _i, _h)))
for _ep in range(4):
    for _m in range(8):
        for _ii in range(2):
            T_GAPS.append((44 + 16 * _ep + 2 * _m + _ii, ("exp", _ep, _m, _ii)))
T_INV = {}
for _g, _item in T_GAPS:
    T_INV.setdefault(_g, []).append(_item)

V_INS = []
for _ep in range(4):
    V_INS.append((44 + 16 * _ep, ("hg", _ep)))
for _ep in range(4):
    for _m in range(8):
        V_INS.append((46 + 2 * (8 * _ep + _m), ("eadd", _ep, _m)))
_SOADD_TILES = {0: 38, 1: 66, 2: 98, 3: 128}
for _i in range(4):
    V_INS.append((_SOADD_TILES[_i], ("soadd", _i)))
V_INS.sort(key=lambda x: x[0])

A_EVS = []
for _jj in range(16):
    A_EVS.append((5 + 2 * _jj, ("af", _jj)))
for _i in range(4):
    A_EVS.append((6 + 2 * _i, ("wdf", _i)))
for _e in range(8):
    A_EVS.append((16 + _e, ("gb", _e)))
A_EVS.sort(key=lambda x: x[0])

P_INS = [(7 + 2 * _jj, _jj) for _jj in range(16)]


def _item_pieces(item):
    if item[0] == "si":
        jj, h = item[1], item[2]
        return [(24 + 2 * jj + h) if jj < 8 else (40 + 2 * (jj - 8) + h)]
    if item[0] == "so":
        return [56 + 2 * item[1] + item[2]]
    if item[0] == "exp":
        ep, m, ii = item[1], item[2], item[3]
        return [64 + 16 * ep + 2 * m + ii]
    return []


MAIN_ORDER = []
SWF_AFTER = {}
_cnt = 24
for _gap in sorted(T_INV):
    for _item in T_INV[_gap]:
        ps = _item_pieces(_item)
        MAIN_ORDER.extend(ps)
        _cnt += len(ps)
        SWF_AFTER[_item[:3] if _item[0] == "exp" else _item] = _cnt
assert _cnt == 128 and len(MAIN_ORDER) == 104
STREAM_ORDER = list(range(24)) + MAIN_ORDER
POS = {p: i for i, p in enumerate(STREAM_ORDER)}


def _pk(w, kc, mblocks):
    K, M = w.shape
    assert K == kc * 128 and M == mblocks * 128
    return np.ascontiguousarray(w.reshape(kc, 128, mblocks, 128).transpose(1, 0, 2, 3))


def pack_inputs(inp):
    f = np.float32
    out = {}
    wip = np.ascontiguousarray(np.asarray(inp["L0_attn_in_proj_w"], f).T)
    wdl = np.ascontiguousarray(np.asarray(inp["L0_attn_delta_proj_w"], f).T)
    wbp = np.ascontiguousarray(np.asarray(inp["L0_attn_b_proj_w"], f).T)
    wsi = np.asarray(inp["L0_ffn_w_shared_in"], f)
    ww1 = np.asarray(inp["L0_ffn_w1_shared"], f)
    wso = np.asarray(inp["L0_ffn_w_shared_out"], f)
    ww2 = np.asarray(inp["L0_ffn_w2"], f)
    wgt = np.asarray(inp["L0_ffn_w_gate"], f)

    pieces = []
    ipk = _pk(wip, NK, 16)
    for mb in range(16):
        pieces.append(ipk[:, :, mb, :])
    dpk = _pk(wdl, NK, 8)
    for mb in range(8):
        pieces.append(dpk[:, :, mb, :])
    sipk = _pk(wsi, NK, 16)
    for mb in SI_ORDER:
        pieces.append(sipk[:, :, mb, :])
    w1pk = _pk(ww1, NK, 16)
    for mb in SI_ORDER:
        pieces.append(w1pk[:, :, mb, :])
    sopk = _pk(wso, NK, 8)
    for mb in range(8):
        pieces.append(sopk[:, :, mb, :])
    w2pk = [_pk(ww2[e], NK, 8) for e in range(E)]
    for ep in range(4):
        for m in range(8):
            for e in (2 * ep, 2 * ep + 1):
                pieces.append(w2pk[e][:, :, m, :])
    assert len(pieces) == N_PIECE
    pieces = [pieces[p] for p in STREAM_ORDER]
    out["wstream"] = np.ascontiguousarray(np.stack(pieces, axis=1))

    out["wbp"] = np.ascontiguousarray(wbp.reshape(NK, 128, 4, 32).transpose(1, 0, 2, 3))
    out["wgt"] = np.ascontiguousarray(wgt.reshape(NK, 128, 8).transpose(1, 0, 2))
    cpar_full = np.asarray(inp["L0_attn_c_param"], f)
    csel = np.zeros((8, 32, 16, 128), f)
    for r in range(128):
        g_, n_ = divmod(r, 16)
        csel[g_, r % 32, n_, :] = cpar_full[g_ * 128:(g_ + 1) * 128, n_]
    out["csel"] = csel
    out["selc8"] = np.ascontiguousarray(
        np.broadcast_to(np.eye(32, dtype=f)[:, :8][:, :, None], (32, 8, 128)))
    out["ident"] = np.eye(128, dtype=f)
    out["bip"] = np.ascontiguousarray(
        np.asarray(inp["L0_attn_in_proj_b"], f).reshape(16, 128).T)
    out["bbp"] = np.ascontiguousarray(
        np.asarray(inp["L0_attn_b_proj_b"], f).reshape(4, 32).T)
    out["bgt"] = np.asarray(inp["L0_ffn_gate_bias"], f).reshape(8, 1)
    out["bcv"] = np.ascontiguousarray(
        np.asarray(inp["L0_attn_conv_b"], f).reshape(8, 128).T)
    out["wcv"] = np.ascontiguousarray(
        np.asarray(inp["L0_attn_conv_w"], f)[:, 0, :].reshape(8, 128, KCV)
        .transpose(1, 0, 2))
    out["alog"] = np.ascontiguousarray(
        np.asarray(inp["L0_attn_a_log"], f).reshape(8, 128, NST).transpose(1, 0, 2))
    return out


def build():
    nc = bass.Bass("TRN2", target_bir_lowering=False)

    x_h = nc.dram_tensor("x", [2, 2048, D], F32, kind="ExternalInput").ap()
    ws_h = nc.dram_tensor("wstream", [128, N_PIECE, 8, 128], F32,
                          kind="ExternalInput").ap()
    wbp_h = nc.dram_tensor("wbp", [128, 8, 4, 32], F32, kind="ExternalInput").ap()
    wgt_h = nc.dram_tensor("wgt", [128, 8, 8], F32, kind="ExternalInput").ap()
    csel_h = nc.dram_tensor("csel", [8, 32, 16, 128], F32, kind="ExternalInput").ap()
    selc8_h = nc.dram_tensor("selc8", [32, 8, 128], F32, kind="ExternalInput").ap()
    ident_h = nc.dram_tensor("ident", [128, 128], F32, kind="ExternalInput").ap()
    bip_h = nc.dram_tensor("bip", [128, 16], F32, kind="ExternalInput").ap()
    bbp_h = nc.dram_tensor("bbp", [32, 4], F32, kind="ExternalInput").ap()
    bgt_h = nc.dram_tensor("bgt", [8, 1], F32, kind="ExternalInput").ap()
    bcv_h = nc.dram_tensor("bcv", [128, 8], F32, kind="ExternalInput").ap()
    wcv_h = nc.dram_tensor("wcv", [128, 8, KCV], F32, kind="ExternalInput").ap()
    alog_h = nc.dram_tensor("alog", [128, 8, NST], F32, kind="ExternalInput").ap()

    out_h = nc.dram_tensor("out", [2, 2048, D], F32, kind="ExternalOutput").ap()
    wsb_h = nc.dram_tensor("wsbf", [128, N_PIECE, 8, 128], BF16, kind="Internal").ap()
    csb_h = nc.dram_tensor("cselbf", [8, 32, 16, 128], BF16, kind="Internal").ap()

    with ExitStack() as ctx:
        en = ctx.enter_context

        def T(name, shape, dt):
            return en(nc.sbuf_tensor(name, shape, dt))

        env = {}
        ring = env["ring"] = T("ring", [128, RING, 8, 128], BF16)
        x_tok = env["x_tok"] = T("x_tok", [128, 4, D], F32)
        xnp = env["xnp"] = T("xnp", [128, 2, 128], F32)
        xn_fm = env["xn_fm"] = T("xn_fm", [128, 8, TC], BF16)
        u_pad = env["u_pad"] = T("u_pad", [128, 8, 515], BF16)
        v_s = env["v_s"] = T("v_s", [128, 8, TC], BF16)
        dlt = env["dlt"] = T("dlt", [128, 8, TC], BF16)
        sp_t = env["sp_t"] = T("sp_t", [128, 2, TC], F32)
        uc = env["uc"] = T("uc", [128, 2, TC], F32)
        du = env["du"] = T("du", [128, 8, TC], BF16)
        b_sb = env["b_sb"] = T("b_sb", [32, 4, TC], BF16)
        dA = env["dA"] = T("dA", [128, 4, SCP], F32)
        dBu = env["dBu"] = T("dBu", [128, 2, SCP], BF16)
        hsc0 = env["hsc0"] = T("hsc0", [128, SCP], BF16)
        stt_t = env["stt_t"] = T("state", [128, 128], F32)
        ccr = env["ccr"] = T("ccr", [128, 8, 3], BF16)
        ysc = env["ysc"] = T("ysc", [128, 2, TC], F32)
        acc = env["acc"] = T("acc", [128, 8, TC], F32)
        racc = env["racc"] = T("racc", [128, 8, TC], F32)
        accb = env["accb"] = T("accb", [128, 2, 128], F32)
        a_s = env["a_s"] = T("a_s", [128, 2, TC], BF16)
        b_bf = env["b_bf"] = T("b_bf", [128, 2, TC], BF16)
        smid = env["smid"] = T("smid", [128, 8, TC], BF16)
        hid = env["hid"] = T("hid", [128, 8, TC], BF16)
        gb = env["gb"] = T("gb", [128, 8, TC], BF16)
        hg = env["hg"] = T("hg", [128, 2, 8, TC], BF16)
        hsc1 = env["hsc1"] = T("hsc1", [128, SCP], BF16)
        lg = env["lg"] = T("lg", [32, TC], F32)
        wdf = env["wdf"] = T("wdf", [32, TC], F32)
        gt_tok = env["gt_tok"] = T("gt_tok", [128, 4, 8], F32)
        gpr = env["gpr"] = T("gpr", [128, 4, 8], F32)
        gtmp = env["gtmp"] = T("gtmp", [128, 4, 8], F32)
        gcols = env["gcols"] = T("gcols", [128, 4, 4], F32)
        stats = env["stats"] = T("stats", [128, 24], F32)
        csel_sb = env["csel_sb"] = T("csel_sb", [32, 2, 16, 128], BF16)
        wbp_sb = env["wbp_sb"] = T("wbp_sb", [128, 8, 4, 32], BF16)
        wgt_sb = env["wgt_sb"] = T("wgt_sb", [128, 8, 8], BF16)
        selc8_sb = env["selc8_sb"] = T("selc8_sb", [32, 8, 128], F32)
        ident3 = env["ident3"] = T("ident3", [128, 128], F32)
        bip_sb = env["bip_sb"] = T("bip_sb", [128, 16], F32)
        bbp_sb = env["bbp_sb"] = T("bbp_sb", [32, 4], F32)
        bgt_sb = env["bgt_sb"] = T("bgt_sb", [8, 1], F32)
        bcv_sb = env["bcv_sb"] = T("bcv_sb", [128, 8], F32)
        wcv_sb = env["wcv_sb"] = T("wcv_sb", [128, 8, KCV], F32)
        A_sb = env["A_sb"] = T("A_sb", [128, 8, NST], F32)

        pAB = env["pAB"] = en(nc.psum_tensor("pAB", [128, 2, TC], F32))
        pFF = env["pFF"] = en(nc.psum_tensor("pFF", [128, 2, TC], F32))
        pBCb = env["pBCb"] = en(nc.psum_tensor("pBCb", [128, 2, TC], F32))
        pEG = env["pEG"] = en(nc.psum_tensor("pEG", [128, 2, TC], F32))
        env["csb_h"] = csb_h

        sems = {}
        for nm in ["sx", "sring0", "sring1", "swf", "sout", "spro", "spro_s", "spro_c",
                   "smsc",
                   "st_xp", "sa_xe", "sv_ln", "sv_st", "sa_st",
                   "sa_g", "st_bp", "sa_bp",
                   "sa_cv", "sv_cv", "sp_cv",
                   "st_gt", "sv_gt", "sa_gt",
                   "st_bc", "sv_bc", "sa_da", "sv_sc", "sp_y",
                   "sa_f", "sa_b", "sp_f", "sv_f", "st_gb", "sa_gb",
                   "sv_em", "sv_hg",
                   "st_op", "sv_oc", "sv_oa", "sch_a", "sch_v", "s_cs"]:
            sems[nm] = en(nc.semaphore(nm))
        S = type("SS", (), sems)

        pull_next = [0]
        chain_a = [0]
        chain_v = [0]

        def a_link(ins):
            chain_a[0] += 1
            ins.then_inc(S.sch_a, 1)

        def a_wait(eng):
            eng.wait_ge(S.sch_a, chain_a[0])

        def v_link(ins):
            chain_v[0] += 1
            ins.then_inc(S.sch_v, 1)

        def v_wait(eng):
            eng.wait_ge(S.sch_v, chain_v[0])

        def emit_pulls(s, upto_consumed):
            while pull_next[0] < NPULL * NCH:
                gp = pull_next[0]
                gate = PULL * (gp + 1) - RING
                if gate > upto_consumed:
                    break
                c_, j_ = divmod(gp, NPULL)
                base_piece = j_ * PULL
                if gp < NPULL:
                    s.wait_ge(S.spro_c, 16 * 9)
                if gate > 0:
                    s.wait_ge(S.swf, gate)
                sr = S.sring0 if gp % 2 == 0 else S.sring1
                if gp > 1:
                    s.wait_ge(sr, 16 * (gp // 2))
                s.dma_start(
                    ring[:, base_piece % RING:base_piece % RING + PULL, :, :],
                    wsb_h[:, base_piece:base_piece + PULL, :, :],
                ).then_inc(sr, 16)
                pull_next[0] += 1

        def ring_piece(p):
            return ring[:, POS[p] % RING, :, :]

        def wait_piece(t, c, p):
            gp = c * NPULL + POS[p] // PULL
            t.wait_ge(S.sring0 if gp % 2 == 0 else S.sring1, 16 * (gp // 2 + 1))

        env.update(ring_piece=ring_piece, wait_piece=wait_piece,
                   emit_pulls=emit_pulls, v_link=v_link, v_wait=v_wait,
                   a_link=a_link, a_wait=a_wait)

        with nc.Block("prog") as blk:
            @blk.gpsimd
            def _(g):
                for j in range(8):
                    g.dma_start(wsb_h[:, 16 * j:16 * (j + 1), :, :],
                                ws_h[:, 16 * j:16 * (j + 1), :, :]
                                ).then_inc(S.spro_c, 16)
                g.dma_start(csb_h, csel_h).then_inc(S.spro_c, 16)
                g.dma_start(wbp_sb[:], wbp_h).then_inc(S.spro, 16)
                g.dma_start(wgt_sb[:], wgt_h).then_inc(S.spro, 16)

            @blk.sync
            def _(s):
                s.dma_start(selc8_sb[:], selc8_h).then_inc(S.spro_s, 16)
                s.dma_start(ident3[:], ident_h).then_inc(S.spro_s, 16)
                s.dma_start(bip_sb[:], bip_h).then_inc(S.spro_s, 16)
                s.dma_start(bbp_sb[:], bbp_h).then_inc(S.spro_s, 16)
                s.dma_start(bgt_sb[:], bgt_h).then_inc(S.spro_s, 16)
                s.dma_start(bcv_sb[:], bcv_h).then_inc(S.spro_s, 16)
                s.dma_start(wcv_sb[:], wcv_h).then_inc(S.spro_s, 16)
                s.dma_start(A_sb[:], alog_h).then_inc(S.spro_s, 16)
                for h in range(2):
                    s.dma_start(
                        x_tok[:, 2 * h:2 * h + 2, :],
                        x_h[0:1, 256 * h:256 * h + 256, :].squeeze(0)
                        .rearrange("(q p) d -> p q d", p=128),
                    ).then_inc(S.sx, 16)
                emit_pulls(s, 0)

            @blk.scalar
            def _(sc):
                sc.wait_ge(S.spro, 16 * 2)
                sc.wait_ge(S.spro_s, 16 * 8)
                sc.activation(A_sb[:], A_sb[:], AF.Exp).then_inc(S.smsc, 1)
                sc.wait_ge(S.smsc, 1)
                sc.mul(A_sb[:], A_sb[:], -1.0).then_inc(S.smsc, 1)

            for c in range(NCH):
                b, tseg = divmod(c, 4)
                t0 = tseg * TC

                @blk.sync
                def _(s, c=c, b=b, t0=t0):
                    if c > 0:
                        for h in range(2):
                            s.wait_ge(S.sout, 32 * (c - 1) + 16 * (h + 1))
                            s.dma_start(
                                x_tok[:, 2 * h:2 * h + 2, :],
                                x_h[b:b + 1, t0 + 256 * h:t0 + 256 * h + 256, :]
                                .squeeze(0).rearrange("(q p) d -> p q d", p=128),
                            ).then_inc(S.sx, 16)
                    emit_pulls(s, 128 * c)

                @blk.vector
                def _(v, c=c):
                    v.wait_ge(S.sx, 32 * (c + 1))
                    for i in range(4):
                        if i >= 1:
                            v.wait_ge(S.sv_st, C_SVST * c + i)
                        v.tensor_scalar(uc[:].rearrange("p a b -> p (a b)"),
                                        x_tok[:, i, :], 1.0, 0.0,
                                        ALU.mult, ALU.add,
                                        accum_out=stats[:, i:i + 1]
                                        ).then_inc(S.sv_st, 1)
                    v.wait_ge(S.sv_st, C_SVST * c + 4)
                    v.wait_ge(S.sa_st, C_SAST * c + 4)
                    v.tensor_scalar(stats[:, 8:12], stats[:, 0:4], 1.0 / D,
                                    None, ALU.mult).then_inc(S.sv_st, 1)
                    v.tensor_scalar(stats[:, 12:16], stats[:, 4:8], 1.0 / D,
                                    1e-5, ALU.mult, ALU.add).then_inc(S.sv_st, 1)
                    v.wait_ge(S.sv_st, C_SVST * c + 6)
                    v.tensor_tensor(stats[:, 16:20], stats[:, 8:12],
                                    stats[:, 8:12], ALU.mult).then_inc(S.sv_st, 1)
                    v.wait_ge(S.sv_st, C_SVST * c + 7)
                    v.tensor_tensor(stats[:, 12:16], stats[:, 12:16],
                                    stats[:, 16:20], ALU.subtract
                                    ).then_inc(S.sv_st, 1)
                    v.wait_ge(S.sa_st, C_SAST * c + 6)
                    for j in range(32):
                        i, gg = divmod(j, 8)
                        if j >= 2:
                            v.wait_ge(S.st_xp, C_STXP * c + j - 1)
                        elif c > 0:
                            v.wait_ge(S.st_xp, C_STXP * c)
                        v.tensor_scalar(
                            xnp[:, j % 2, :],
                            x_tok[:, i, gg * 128:(gg + 1) * 128],
                            stats[:, 8 + i:9 + i], stats[:, 12 + i:13 + i],
                            ALU.subtract, ALU.mult).then_inc(S.sv_ln, 1)

                @blk.scalar
                def _(sc, c=c):
                    sc.wait_ge(S.sx, 32 * (c + 1))
                    for i in range(4):
                        if i >= 1:
                            sc.wait_ge(S.sa_st, C_SAST * c + i)
                        sc.activation(sp_t[:].rearrange("p a b -> p (a b)"),
                                      x_tok[:, i, :], AF.Square,
                                      accum_out=stats[:, 4 + i:5 + i]
                                      ).then_inc(S.sa_st, 1)
                    sc.wait_ge(S.sv_st, C_SVST * c + 8)
                    sc.activation(stats[:, 20:24], stats[:, 12:16], AF.Ln
                                  ).then_inc(S.sa_st, 1)
                    sc.wait_ge(S.sa_st, C_SAST * c + 5)
                    sc.activation(stats[:, 12:16], stats[:, 20:24], AF.Exp,
                                  scale=-0.5).then_inc(S.sa_st, 1)
                    for j in range(32):
                        i, gg = divmod(j, 8)
                        sc.wait_ge(S.st_xp, C_STXP * c + j + 1)
                        sc.activation(xn_fm[:, gg, i * 128:(i + 1) * 128],
                                      pBCb[:, j % 2, 0:128], AF.Copy
                                      ).then_inc(S.sa_xe, 1)

                @blk.tensor
                def _(t, c=c):
                    if c == 0:
                        t.wait_ge(S.spro, 16 * 2)
                        t.wait_ge(S.spro_s, 16 * 8)
                    for j in range(32):
                        t.wait_ge(S.sv_ln, C_SVLN * c + j + 1)
                        if j >= 2:
                            t.wait_ge(S.sa_xe, C_SAXE * c + j - 1)
                        elif c > 0:
                            t.wait_ge(S.sv_oa, C_SVOA * c)
                        t.transpose(pBCb[:, j % 2, 0:128], xnp[:, j % 2, :],
                                    ident3[:]).then_inc(S.st_xp, 1)

                @blk.tensor
                def _(t, c=c):
                    t.wait_ge(S.sa_xe, C_SAXE * c + 32)
                    if c > 0:
                        t.wait_ge(S.sv_f, C_SVF * c)
                        t.wait_ge(S.sa_f, C_SAF * c)
                        t.wait_ge(S.sa_b, C_SAB * c)
                    for i in range(8):
                        pp = pAB if i % 2 == 0 else pFF
                        if i >= 2:
                            t.wait_ge(S.sa_g, C_SAG * c + i - 1)
                        for half in range(2):
                            mb = 2 * i + half
                            wait_piece(t, c, mb)
                            for kc in range(NK):
                                ins = t.matmul(pp[:, half, :],
                                               ring_piece(mb)[:, kc, :],
                                               xn_fm[:, kc, :],
                                               start=(kc == 0),
                                               stop=(kc == NK - 1))
                            ins.then_inc(S.swf, 1)
                    for i in range(4):
                        gi = 8 + i
                        pp = pAB if gi % 2 == 0 else pFF
                        t.wait_ge(S.sa_g, C_SAG * c + gi - 1)
                        for half in range(2):
                            p_idx = 16 + 2 * i + half
                            wait_piece(t, c, p_idx)
                            for kc in range(NK):
                                ins = t.matmul(pp[:, half, :],
                                               ring_piece(p_idx)[:, kc, :],
                                               u_pad[:, kc, 3:3 + TC],
                                               start=(kc == 0),
                                               stop=(kc == NK - 1))
                            ins.then_inc(S.swf, 1)
                    base_bp = C_STBP * c
                    if c > 0:
                        t.wait_ge(S.sv_em, C_SVEM * c)
                        t.wait_ge(S.sa_gb, C_SAGB * c)
                    for s_ in range(4):
                        if s_ >= 2:
                            t.wait_ge(S.sa_bp, base_bp + s_ - 1)
                        for kc in range(NK):
                            ins = t.matmul(pEG[0:32, s_ % 2, :],
                                           wbp_sb[:, kc, s_, :],
                                           u_pad[:, kc, 3:3 + TC],
                                           start=(kc == 0), stop=(kc == NK - 1))
                        ins.then_inc(S.st_bp, 1)
                    t.wait_ge(S.sa_bp, base_bp + 3)
                    for kc in range(NK):
                        ins = t.matmul(pEG[0:8, 0, :], wgt_sb[:, kc, :],
                                       xn_fm[:, kc, :],
                                       start=(kc == 0), stop=(kc == NK - 1))
                    ins.then_inc(S.st_bp, 1)

                @blk.scalar
                def _(sc, c=c):
                    def conv_uc(gg):
                        if gg == 0:
                            sc.wait_ge(S.sv_cv, C_SVCV * c + 1)
                            if c > 0:
                                sc.wait_ge(S.sv_bc, C_SVBC * c)
                        sc.activation(uc[:, gg % 2, :], u_pad[:, gg, 3:3 + TC],
                                      AF.Identity, scale=wcv_sb[:, gg, 3:4]
                                      ).then_inc(S.sa_cv, 1)

                    def conv_silu(gg):
                        sc.wait_ge(S.sv_cv, C_SVCV * c + 2 * gg + 3)
                        sc.activation(du[:, gg, :], uc[:, gg % 2, :], AF.Silu,
                                      bias=bcv_sb[:, gg:gg + 1]).then_inc(S.sa_cv, 1)

                    for i in range(8):
                        pp = pAB if i % 2 == 0 else pFF
                        sc.wait_ge(S.swf, 128 * c + 2 * i + 2)
                        if i == 0 and c > 0:
                            sc.wait_ge(S.sv_cv, C_SVCV * c)
                        if i == 4 and c > 0:
                            sc.wait_ge(S.sp_y, C_SPY * c)
                        for half in range(2):
                            mb = 2 * i + half
                            if mb < 8:
                                ins = sc.activation(u_pad[:, mb, 3:515],
                                                    pp[:, half, :], AF.Identity,
                                                    bias=bip_sb[:, mb:mb + 1])
                            else:
                                ins = sc.activation(v_s[:, mb - 8, :],
                                                    pp[:, half, :], AF.Silu,
                                                    bias=bip_sb[:, mb:mb + 1])
                        ins.then_inc(S.sa_g, 1)
                        if i >= 1:
                            conv_silu(i - 1)
                        conv_uc(i)
                    conv_silu(7)
                    for i in range(4):
                        gi = 8 + i
                        pp = pAB if gi % 2 == 0 else pFF
                        sc.wait_ge(S.swf, 128 * c + 16 + 2 * i + 2)
                        if i >= 1:
                            sc.wait_ge(S.sa_g, C_SAG * c + 8 + i)
                        if i == 0 and c > 0:
                            sc.wait_ge(S.sp_cv, C_SPCV * c)
                        a_link(sc.activation(sp_t[:], pp[:, :, :], AF.Exp))
                        a_wait(sc)
                        sc.activation(dlt[:, 2 * i:2 * i + 2, :], sp_t[:],
                                      AF.Ln, bias=1.0).then_inc(S.sa_g, 1)
                    base_bp = C_STBP * c
                    for s_ in range(4):
                        sc.wait_ge(S.st_bp, base_bp + s_ + 1)
                        sc.activation(b_sb[:, s_, :], pEG[0:32, s_ % 2, :],
                                      AF.Identity, bias=bbp_sb[:, s_:s_ + 1]
                                      ).then_inc(S.sa_bp, 1)
                    sc.wait_ge(S.st_bp, base_bp + 5)
                    sc.activation(lg[0:8, :], pEG[0:8, 0, :], AF.Identity,
                                  bias=bgt_sb[:]).then_inc(S.sa_bp, 1)

                @blk.sync
                def _(s, c=c):
                    emit_pulls(s, 128 * c + 24)

                emit_main(nc, blk, S, c, tseg, env)

                @blk.vector
                def _(v, c=c):
                    for j in range(32):
                        i, gg = divmod(j, 8)
                        if j >= 2:
                            v.wait_ge(S.st_op, C_STOP * c + j - 1)
                        elif c > 0:
                            v.wait_ge(S.st_op, C_STOP * c)
                        v.tensor_tensor(accb[:, j % 2, :],
                                        acc[:, gg, i * 128:(i + 1) * 128],
                                        racc[:, gg, i * 128:(i + 1) * 128],
                                        ALU.add).then_inc(S.sv_oc, 1)
                        v.wait_ge(S.st_op, C_STOP * c + j + 1)
                        v.tensor_tensor(x_tok[:, i, gg * 128:(gg + 1) * 128],
                                        x_tok[:, i, gg * 128:(gg + 1) * 128],
                                        pBCb[:, j % 2, 0:128], ALU.add
                                        ).then_inc(S.sv_oa, 1)

                @blk.tensor
                def _(t, c=c):
                    for j in range(32):
                        t.wait_ge(S.sv_oc, C_SVOC * c + j + 1)
                        if j >= 2:
                            t.wait_ge(S.sv_oa, C_SVOA * c + j - 1)
                        elif j == 0:
                            t.wait_ge(S.sv_bc, C_SVBC * c + 128)
                        t.transpose(pBCb[:, j % 2, 0:128], accb[:, j % 2, :],
                                    ident3[:]).then_inc(S.st_op, 1)

                @blk.sync
                def _(s, c=c, b=b, t0=t0):
                    for h in range(2):
                        s.wait_ge(S.sv_oa, C_SVOA * c + 16 * (h + 1))
                        s.dma_start(
                            out_h[b:b + 1, t0 + 256 * h:t0 + 256 * h + 256, :]
                            .squeeze(0).rearrange("(q p) d -> p q d", p=128),
                            x_tok[:, 2 * h:2 * h + 2, :],
                        ).then_inc(S.sout, 16)

            @blk.sync
            def _(s):
                s.wait_ge(S.sout, 32 * NCH)

    return nc


def emit_main(nc, blk, S, c, tseg, env):
    ring_piece = env["ring_piece"]; wait_piece = env["wait_piece"]
    emit_pulls = env["emit_pulls"]
    v_link = env["v_link"]; v_wait = env["v_wait"]
    a_link = env["a_link"]; a_wait = env["a_wait"]
    u_pad = env["u_pad"]; v_s = env["v_s"]; dlt = env["dlt"]; uc = env["uc"]
    du = env["du"]; b_sb = env["b_sb"]; dA = env["dA"]; dBu = env["dBu"]
    hsc0 = env["hsc0"]; hsc1 = env["hsc1"]
    stt_t = env["stt_t"]; ccr = env["ccr"]; ysc = env["ysc"]
    acc = env["acc"]; racc = env["racc"]; a_s = env["a_s"]; b_bf = env["b_bf"]
    smid = env["smid"]; hid = env["hid"]; gb = env["gb"]; hg = env["hg"]
    lg = env["lg"]; wdf = env["wdf"]; gt_tok = env["gt_tok"]; gpr = env["gpr"]
    gtmp = env["gtmp"]; gcols = env["gcols"]
    csel_sb = env["csel_sb"]; selc8_sb = env["selc8_sb"]; ident3 = env["ident3"]
    wcv_sb = env["wcv_sb"]; bcv_sb = env["bcv_sb"]; A_sb = env["A_sb"]
    xn_fm = env["xn_fm"]
    pAB = env["pAB"]; pBCb = env["pBCb"]; pEG = env["pEG"]; pFF = env["pFF"]
    csb_h = env["csb_h"]

    @blk.sync
    def _(s):
        def csel_dma(gg):
            gidx = 8 * c + gg
            if gg >= 2:
                s.wait_ge(S.st_bc, C_STBC * c + 16 * (gg - 1))
            elif c > 0:
                s.wait_ge(S.st_bc, C_STBC * (c - 1) + 112 + 16 * gg)
            else:
                s.wait_ge(S.spro_c, 16 * 9)
            if gidx > 0:
                s.wait_ge(S.s_cs, 16 * gidx)
            s.dma_start(csel_sb[:, gg % 2, :, :], csb_h[gg]).then_inc(S.s_cs, 16)

        csel_dma(0)
        csel_dma(1)
        for gg in range(2, 8):
            emit_pulls(s, 128 * c + 16 * gg + 4)
            csel_dma(gg)
        emit_pulls(s, 128 * c + 128)

    @blk.scalar
    def _(sc):
        for i in range(4):
            sc.wait_ge(S.sv_gt, C_SVGT * c + 12 * i + 3)
            sc.activation(gpr[:, i, :], gt_tok[:, i, :], AF.Exp,
                          bias=gcols[:, i, 0:1]).then_inc(S.sa_gt, 1)
        ai = 0
        for j in range(128):
            gg, n = divmod(j, NST)
            if j >= 4:
                sc.wait_ge(S.sv_sc, C_SVSC * c + j - 3)
            elif c > 0:
                sc.wait_ge(S.sv_sc, C_SVSC * c)
            sc.activation(dA[:, j % 4, 0:TC], dlt[:, gg, :], AF.Exp,
                          scale=A_sb[:, gg, n:n + 1]).then_inc(S.sa_da, 1)
            while ai < len(A_EVS) and A_EVS[ai][0] <= j:
                kind = A_EVS[ai][1]
                ai += 1
                if kind[0] == "af":
                    jj = kind[1]
                    pp = pAB if jj % 2 == 0 else pFF
                    sc.wait_ge(S.swf, 128 * c + SWF_AFTER[("si", jj, 0)])
                    if jj >= 2:
                        sc.wait_ge(S.sp_f, C_SPF * c + jj - 1)
                    elif c > 0:
                        sc.wait_ge(S.sp_f, C_SPF * c)
                    sc.activation(a_s[:, jj % 2, :], pp[:, 0, :], AF.Silu
                                  ).then_inc(S.sa_f, 1)
                    sc.wait_ge(S.swf, 128 * c + SWF_AFTER[("si", jj, 1)])
                    sc.activation(b_bf[:, jj % 2, :], pp[:, 1, :], AF.Copy
                                  ).then_inc(S.sa_b, 1)
                elif kind[0] == "wdf":
                    i = kind[1]
                    sc.wait_ge(S.st_gt, C_STGT * c + 4 + i + 1)
                    sc.activation(wdf[0:8, i * 128:(i + 1) * 128],
                                  pEG[0:8, i % 2, 8:136], AF.Copy
                                  ).then_inc(S.sa_gt, 1)
                else:
                    e = kind[1]
                    sc.wait_ge(S.st_gb, C_STGB * c + e + 1)
                    if e == 0 and c > 0:
                        sc.wait_ge(S.sv_hg, C_SVHG * c)
                    sc.activation(gb[:, e, :], pEG[:, e % 2, :], AF.Copy
                                  ).then_inc(S.sa_gb, 1)
        assert ai == len(A_EVS)

    @blk.vector
    def _(v):
        if tseg == 0:
            v.memset(u_pad[:, :, 0:3], 0.0).then_inc(S.sv_cv, 1)
        else:
            v.tensor_copy(u_pad[:, :, 0:3], ccr[:]).then_inc(S.sv_cv, 1)
        for gg in range(8):
            v.wait_ge(S.sa_g, C_SAG * c + gg // 2 + 1)
            v.tensor_copy(ccr[:, gg, :], u_pad[:, gg, 512:515]
                          ).then_inc(S.sv_cv, 1)
            v.wait_ge(S.sa_cv, C_SACV * c + 2 * gg + 1)
            for k in range(3):
                if k > 0:
                    v_wait(v)
                ins = v.scalar_tensor_tensor(
                    uc[:, gg % 2, :], u_pad[:, gg, k:k + TC],
                    wcv_sb[:, gg, k:k + 1], uc[:, gg % 2, :],
                    ALU.mult, ALU.add)
                if k < 2:
                    v_link(ins)
            ins.then_inc(S.sv_cv, 1)
        k = C_SVGT * c
        for i in range(4):
            v.wait_ge(S.st_gt, C_STGT * c + i + 1)
            v.tensor_copy(gt_tok[:, i, :], pEG[:, i % 2, 0:8]
                          ).then_inc(S.sv_gt, 1); k += 1
            v.wait_ge(S.sv_gt, k)
            v.tensor_reduce(gcols[:, i, 0:1], gt_tok[:, i, :], AX.X,
                            ALU.max).then_inc(S.sv_gt, 1); k += 1
            v.wait_ge(S.sv_gt, k)
            v.tensor_scalar(gcols[:, i, 0:1], gcols[:, i, 0:1], -1.0,
                            None, ALU.mult).then_inc(S.sv_gt, 1); k += 1
            v.wait_ge(S.sv_gt, k)
            v.wait_ge(S.sa_gt, C_SAGT * c + i + 1)
            v.tensor_reduce(gcols[:, i, 1:2], gpr[:, i, :], AX.X,
                            ALU.add).then_inc(S.sv_gt, 1); k += 1
            v.wait_ge(S.sv_gt, k)
            v.reciprocal(gcols[:, i, 2:3], gcols[:, i, 1:2]).then_inc(S.sv_gt, 1)
            k += 1
            v.wait_ge(S.sv_gt, k)
            v.tensor_scalar(gpr[:, i, :], gpr[:, i, :], gcols[:, i, 2:3],
                            None, ALU.mult).then_inc(S.sv_gt, 1); k += 1
            v.wait_ge(S.sv_gt, k)
            v.tensor_reduce(gcols[:, i, 0:1], gpr[:, i, :], AX.X,
                            ALU.max).then_inc(S.sv_gt, 1); k += 1
            v.wait_ge(S.sv_gt, k)
            v.tensor_scalar(gtmp[:, i, :], gpr[:, i, :], gcols[:, i, 0:1],
                            -1e30, ALU.is_ge, ALU.mult).then_inc(S.sv_gt, 1)
            k += 1
            v.wait_ge(S.sv_gt, k)
            v.tensor_tensor(gtmp[:, i, :], gpr[:, i, :], gtmp[:, i, :],
                            ALU.add).then_inc(S.sv_gt, 1); k += 1
            v.wait_ge(S.sv_gt, k)
            v.tensor_reduce(gcols[:, i, 3:4], gtmp[:, i, :], AX.X,
                            ALU.max).then_inc(S.sv_gt, 1); k += 1
            v.wait_ge(S.sv_gt, k)
            v.tensor_scalar(gtmp[:, i, :], gpr[:, i, :], gcols[:, i, 3:4],
                            None, ALU.is_ge).then_inc(S.sv_gt, 1); k += 1
            v.wait_ge(S.sv_gt, k)
            v.tensor_tensor(gpr[:, i, :], gpr[:, i, :], gtmp[:, i, :],
                            ALU.mult).then_inc(S.sv_gt, 1); k += 1
            v.wait_ge(S.sv_gt, k)

        def v_insert(kind):
            if kind[0] == "hg":
                ep = kind[1]
                if ep >= 1:
                    v.wait_ge(S.swf, 128 * c + SWF_AFTER[("exp", ep - 1, 7)])
                elif c > 0:
                    v.wait_ge(S.swf, 128 * c)
                v.wait_ge(S.sa_gb, C_SAGB * c + 2 * ep + 2)
                v.wait_ge(S.sp_f, C_SPF * c + 16)
                for ii, e in enumerate((2 * ep, 2 * ep + 1)):
                    for kc in range(NK):
                        ins = v.tensor_tensor(hg[:, ii, kc, :], hid[:, kc, :],
                                              gb[:, e, :], ALU.mult)
                ins.then_inc(S.sv_hg, 1)
            elif kind[0] == "soadd":
                i = kind[1]
                pp = pAB if (16 + i) % 2 == 0 else pFF
                v.wait_ge(S.swf, 128 * c + SWF_AFTER[("so", i, 1)])
                ja = 32 * i + 31
                v.wait_ge(S.sp_y, C_SPY * c + ja + ja // 16 + 2)
                v.tensor_tensor(acc[:, 2 * i:2 * i + 2, :],
                                acc[:, 2 * i:2 * i + 2, :], pp[:, :, :],
                                ALU.add).then_inc(S.sv_f, 1)
            else:
                ep, m = kind[1], kind[2]
                q = 8 * ep + m
                v.wait_ge(S.swf, 128 * c + SWF_AFTER[("exp", ep, m)])
                if ep == 0:
                    ins = v.tensor_copy(racc[:, m, :], pEG[:, m % 2, :])
                else:
                    ins = v.tensor_tensor(racc[:, m, :], racc[:, m, :],
                                          pEG[:, m % 2, :], ALU.add)
                ins.then_inc(S.sv_em, 1)

        vi = 0
        for j in range(128):
            gg, n = divmod(j, NST)
            while vi < len(V_INS) and V_INS[vi][0] <= j:
                v_insert(V_INS[vi][1]); vi += 1
            v.wait_ge(S.st_bc, C_STBC * c + j + 1)
            if n == 0:
                v.wait_ge(S.sp_cv, C_SPCV * c + gg + 1)
            if j >= 2:
                v.wait_ge(S.sp_y, C_SPY * c + (j - 2) + (j - 2) // 16 + 1)
            v.tensor_tensor(dBu[:, j % 2, 0:TC], du[:, gg, :],
                            pBCb[:, j % 2, :], ALU.mult).then_inc(S.sv_bc, 1)
            v.wait_ge(S.sa_da, C_SADA * c + j + 1)
            v.wait_ge(S.sv_bc, C_SVBC * c + j + 1)
            init = 0.0 if tseg == 0 else stt_t[:, j:j + 1]
            hs = hsc0 if j % 2 == 0 else hsc1
            v.tensor_tensor_scan(hs[:, 0:TC], dA[:, j % 4, 0:TC],
                                 dBu[:, j % 2, 0:TC], init,
                                 ALU.mult, ALU.add).then_inc(S.sv_sc, 1)
        while vi < len(V_INS):
            v_insert(V_INS[vi][1]); vi += 1

    @blk.gpsimd
    def _(gp):
        for gg in range(8):
            gp.wait_ge(S.sa_cv, C_SACV * c + 2 * gg + 2)
            gp.wait_ge(S.sa_g, C_SAG * c + 8 + gg // 2 + 1)
            gp.tensor_tensor(du[:, gg, :], du[:, gg, :], dlt[:, gg, :],
                             ALU.mult).then_inc(S.sp_cv, 1)
        pi = 0
        for j in range(128):
            gg, n = divmod(j, NST)
            while pi < len(P_INS) and P_INS[pi][0] <= j:
                jj = P_INS[pi][1]
                pi += 1
                gp.wait_ge(S.sa_f, C_SAF * c + jj + 1)
                gp.wait_ge(S.sa_b, C_SAB * c + jj + 1)
                if jj == 8 and c > 0:
                    gp.wait_ge(S.sv_hg, C_SVHG * c)
                dst = smid[:, jj, :] if jj < 8 else hid[:, jj - 8, :]
                gp.tensor_tensor(dst, a_s[:, jj % 2, :], b_bf[:, jj % 2, :],
                                 ALU.mult).then_inc(S.sp_f, 1)
            gp.wait_ge(S.sv_sc, C_SVSC * c + j + 1)
            gp.wait_ge(S.sp_y, C_SPY * c + j + j // 16)
            hs = hsc0 if j % 2 == 0 else hsc1
            gp.tensor_copy(stt_t[:, j:j + 1], hs[:, TC - 1:TC])
            if n == 0:
                ins = gp.tensor_copy(ysc[:, gg % 2, :], hs[:, 0:TC])
            else:
                ins = gp.tensor_tensor(ysc[:, gg % 2, :], hs[:, 0:TC],
                                       ysc[:, gg % 2, :], ALU.add)
            ins.then_inc(S.sp_y, 1)
            if n == NST - 1:
                gp.wait_ge(S.sp_y, C_SPY * c + j + j // 16 + 1)
                gp.tensor_tensor(acc[:, gg, :], ysc[:, gg % 2, :],
                                 v_s[:, gg, :], ALU.mult).then_inc(S.sp_y, 1)
        assert pi == len(P_INS)

    @blk.tensor
    def _(t):
        t.wait_ge(S.sa_bp, C_SABP * c + 5)
        for i in range(4):
            if i >= 2:
                t.wait_ge(S.sv_gt, C_SVGT * c + 12 * (i - 2) + 1)
            t.transpose(pEG[:, i % 2, 0:8], lg[0:8, i * 128:(i + 1) * 128],
                        ident3[0:8, 0:8]).then_inc(S.st_gt, 1)

        def emit_item(item):
            if item[0] == "si":
                jj, half = item[1], item[2]
                pp = pAB if jj % 2 == 0 else pFF
                if jj >= 2 and half == 0:
                    t.wait_ge(S.sa_f, C_SAF * c + jj - 1)
                    t.wait_ge(S.sa_b, C_SAB * c + jj - 1)
                elif jj < 2 and half == 0:
                    t.wait_ge(S.sa_g, C_SAG * c + 12)
                p_idx = (24 + jj * 2 + half) if jj < 8 else (40 + (jj - 8) * 2 + half)
                wait_piece(t, c, p_idx)
                for kc in range(NK):
                    ins = t.matmul(pp[:, half, :], ring_piece(p_idx)[:, kc, :],
                                   xn_fm[:, kc, :],
                                   start=(kc == 0), stop=(kc == NK - 1))
                ins.then_inc(S.swf, 1)
            elif item[0] == "so":
                i, half = item[1], item[2]
                jj = 16 + i
                pp = pAB if jj % 2 == 0 else pFF
                if half == 0:
                    t.wait_ge(S.sa_f, C_SAF * c + min(jj - 1, 16))
                    t.wait_ge(S.sa_b, C_SAB * c + min(jj - 1, 16))
                    if i >= 2:
                        t.wait_ge(S.sv_f, C_SVF * c + i - 1)
                    t.wait_ge(S.sp_f, C_SPF * c + 8)
                p_idx = 56 + 2 * i + half
                wait_piece(t, c, p_idx)
                for kc in range(NK):
                    ins = t.matmul(pp[:, half, :], ring_piece(p_idx)[:, kc, :],
                                   smid[:, kc, :],
                                   start=(kc == 0), stop=(kc == NK - 1))
                ins.then_inc(S.swf, 1)
            elif item[0] == "backT":
                i = item[1]
                t.wait_ge(S.sv_gt, C_SVGT * c + max(12 * i + 12, 37))
                if i >= 2:
                    t.wait_ge(S.sa_gt, C_SAGT * c + 4 + (i - 2) + 1)
                t.transpose(pEG[0:8, i % 2, 8:136], gpr[:, i, :],
                            ident3[:]).then_inc(S.st_gt, 1)
            elif item[0] == "gatebc":
                e = item[1]
                if e < 2:
                    t.wait_ge(S.sa_gt, C_SAGT * c + 8)
                else:
                    t.wait_ge(S.sa_gb, C_SAGB * c + e - 1)
                t.matmul(pEG[:, e % 2, :], selc8_sb[0:8, e, :], wdf[0:8, :],
                         start=True, stop=True).then_inc(S.st_gb, 1)
            else:
                ep, m, ii = item[1], item[2], item[3]
                q = 8 * ep + m
                if ii == 0:
                    if q == 0:
                        t.wait_ge(S.sa_gb, C_SAGB * c + 7)
                    elif q == 1:
                        t.wait_ge(S.sa_gb, C_SAGB * c + 8)
                    if m == 0:
                        t.wait_ge(S.sv_hg, C_SVHG * c + ep + 1)
                    if q >= 2:
                        t.wait_ge(S.sv_em, C_SVEM * c + q - 1)
                p_idx = 64 + 16 * ep + 2 * m + ii
                wait_piece(t, c, p_idx)
                for kc in range(NK):
                    ins = t.matmul(pEG[:, m % 2, :],
                                   ring_piece(p_idx)[:, kc, :],
                                   hg[:, ii, kc, :],
                                   start=(ii == 0 and kc == 0),
                                   stop=(ii == 1 and kc == NK - 1))
                ins.then_inc(S.swf, 1)

        for j in range(128):
            for item in T_INV.get(j, []):
                emit_item(item)
            gg, n = divmod(j, NST)
            s_ = j // 32
            if j >= 2:
                t.wait_ge(S.sv_bc, C_SVBC * c + j - 1)
            if n == 0:
                t.wait_ge(S.s_cs, 16 * (8 * c + gg + 1))
            if j % 32 == 0:
                t.wait_ge(S.sa_bp, C_SABP * c + s_ + 1)
            t.matmul(pBCb[:, j % 2, :], csel_sb[:, gg % 2, n, :],
                     b_sb[:, s_, :], start=True, stop=True).then_inc(S.st_bc, 1)
        for j2 in range(128, 140):
            for item in T_INV.get(j2, []):
                emit_item(item)


def kernel(**inputs):
    shared = pack_inputs(inputs)
    x = np.ascontiguousarray(np.asarray(inputs["L0_attn_x"], np.float32))
    nc = build()
    in_maps = []
    for cidx in range(NCORES):
        m = dict(shared)
        m["x"] = np.ascontiguousarray(x[2 * cidx:2 * cidx + 2])
        in_maps.append(m)
    res = run_bass_kernel_spmd(nc, in_maps, core_ids=list(range(NCORES)))
    return np.concatenate([r["out"] for r in res.results], axis=0)


# version 31 (final)
# speedup vs baseline: 1.0016x; 1.0016x over previous
import numpy as np
from contextlib import ExitStack

import concourse.bass as bass
import concourse.mybir as mybir
from concourse.bass_utils import run_bass_kernel_spmd

F32 = mybir.dt.float32
BF16 = mybir.dt.bfloat16
AF = mybir.ActivationFunctionType
ALU = mybir.AluOpType
AX = mybir.AxisListType

D = 1024
NST = 16
KCV = 4
E = 8
NK = D // 128
TC = 512
NCH = 8
NCORES = 8
SCP = TC

N_PIECE = 128
RING = 8
PULL = 2
NPULL = N_PIECE // PULL

SI_ORDER = [0, 8, 1, 9, 2, 10, 3, 11, 4, 12, 5, 13, 6, 14, 7, 15]

C_SVST = 8
C_SAST = 6
C_STXP = 32
C_SAXE = 32
C_SVLN = 32
C_SAG = 12
C_STBP = 5
C_SABP = 5
C_SACV = 16
C_SVCV = 17
C_SPCV = 8
C_STGT = 8
C_SAGT = 8
C_SVGT = 48
C_STBC = 128
C_SVBC = 128
C_SADA = 128
C_SVSC = 128
C_SPY = 136
C_SAF = 16
C_SAB = 16
C_SPF = 16
C_SVF = 4
C_STGB = 8
C_SAGB = 8
C_SVEM = 32
C_SVHG = 4
C_STOP = 32
C_SVOC = 32
C_SVOA = 32

T_GAPS = []
for _jj in range(16):
    for _h in range(2):
        T_GAPS.append((2 + 2 * _jj + _h, ("si", _jj, _h)))
for _i in range(4):
    T_GAPS.append((4 + 2 * _i, ("backT", _i)))
for _e in range(8):
    T_GAPS.append((14 + _e, ("gatebc", _e)))
_SO_GAPS = {0: 34, 1: 36, 2: 40, 3: 68}
for _i in range(4):
    for _h in range(2):
        T_GAPS.append((_SO_GAPS[_i] + _h, ("so", _i, _h)))
for _ep in range(4):
    for _m in range(8):
        for _ii in range(2):
            T_GAPS.append((44 + 16 * _ep + 2 * _m + _ii, ("exp", _ep, _m, _ii)))
T_INV = {}
for _g, _item in T_GAPS:
    T_INV.setdefault(_g, []).append(_item)

V_INS = []
for _ep in range(4):
    V_INS.append((44 + 16 * _ep, ("hg", _ep)))
for _ep in range(4):
    for _m in range(8):
        V_INS.append((46 + 2 * (8 * _ep + _m), ("eadd", _ep, _m)))
_SOADD_TILES = {0: 38, 1: 66, 2: 98, 3: 128}
for _i in range(4):
    V_INS.append((_SOADD_TILES[_i], ("soadd", _i)))
V_INS.sort(key=lambda x: x[0])

A_EVS = []
for _jj in range(16):
    A_EVS.append((5 + 2 * _jj, ("af", _jj)))
for _i in range(4):
    A_EVS.append((6 + 2 * _i, ("wdf", _i)))
for _e in range(8):
    A_EVS.append((16 + _e, ("gb", _e)))
A_EVS.sort(key=lambda x: x[0])

P_INS = [(7 + 2 * _jj, _jj) for _jj in range(16)]


def _item_pieces(item):
    if item[0] == "si":
        jj, h = item[1], item[2]
        return [(24 + 2 * jj + h) if jj < 8 else (40 + 2 * (jj - 8) + h)]
    if item[0] == "so":
        return [56 + 2 * item[1] + item[2]]
    if item[0] == "exp":
        ep, m, ii = item[1], item[2], item[3]
        return [64 + 16 * ep + 2 * m + ii]
    return []


MAIN_ORDER = []
SWF_AFTER = {}
_cnt = 24
for _gap in sorted(T_INV):
    for _item in T_INV[_gap]:
        ps = _item_pieces(_item)
        MAIN_ORDER.extend(ps)
        _cnt += len(ps)
        SWF_AFTER[_item[:3] if _item[0] == "exp" else _item] = _cnt
assert _cnt == 128 and len(MAIN_ORDER) == 104
STREAM_ORDER = list(range(24)) + MAIN_ORDER
POS = {p: i for i, p in enumerate(STREAM_ORDER)}


def _pk(w, kc, mblocks):
    K, M = w.shape
    assert K == kc * 128 and M == mblocks * 128
    return np.ascontiguousarray(w.reshape(kc, 128, mblocks, 128).transpose(1, 0, 2, 3))


def pack_inputs(inp):
    f = np.float32
    out = {}
    wip = np.ascontiguousarray(np.asarray(inp["L0_attn_in_proj_w"], f).T)
    wdl = np.ascontiguousarray(np.asarray(inp["L0_attn_delta_proj_w"], f).T)
    wbp = np.ascontiguousarray(np.asarray(inp["L0_attn_b_proj_w"], f).T)
    wsi = np.asarray(inp["L0_ffn_w_shared_in"], f)
    ww1 = np.asarray(inp["L0_ffn_w1_shared"], f)
    wso = np.asarray(inp["L0_ffn_w_shared_out"], f)
    ww2 = np.asarray(inp["L0_ffn_w2"], f)
    wgt = np.asarray(inp["L0_ffn_w_gate"], f)

    pieces = []
    ipk = _pk(wip, NK, 16)
    for mb in range(16):
        pieces.append(ipk[:, :, mb, :])
    dpk = _pk(wdl, NK, 8)
    for mb in range(8):
        pieces.append(dpk[:, :, mb, :])
    sipk = _pk(wsi, NK, 16)
    for mb in SI_ORDER:
        pieces.append(sipk[:, :, mb, :])
    w1pk = _pk(ww1, NK, 16)
    for mb in SI_ORDER:
        pieces.append(w1pk[:, :, mb, :])
    sopk = _pk(wso, NK, 8)
    for mb in range(8):
        pieces.append(sopk[:, :, mb, :])
    w2pk = [_pk(ww2[e], NK, 8) for e in range(E)]
    for ep in range(4):
        for m in range(8):
            for e in (2 * ep, 2 * ep + 1):
                pieces.append(w2pk[e][:, :, m, :])
    assert len(pieces) == N_PIECE
    pieces = [pieces[p] for p in STREAM_ORDER]
    out["wstream"] = np.ascontiguousarray(np.stack(pieces, axis=1))

    out["wbp"] = np.ascontiguousarray(wbp.reshape(NK, 128, 4, 32).transpose(1, 0, 2, 3))
    out["wgt"] = np.ascontiguousarray(wgt.reshape(NK, 128, 8).transpose(1, 0, 2))
    cpar_full = np.asarray(inp["L0_attn_c_param"], f)
    csel = np.zeros((8, 32, 16, 128), f)
    for r in range(128):
        g_, n_ = divmod(r, 16)
        csel[g_, r % 32, n_, :] = cpar_full[g_ * 128:(g_ + 1) * 128, n_]
    out["csel"] = csel
    out["selc8"] = np.ascontiguousarray(
        np.broadcast_to(np.eye(32, dtype=f)[:, :8][:, :, None], (32, 8, 128)))
    out["ident"] = np.eye(128, dtype=f)
    out["bip"] = np.ascontiguousarray(
        np.asarray(inp["L0_attn_in_proj_b"], f).reshape(16, 128).T)
    out["bbp"] = np.ascontiguousarray(
        np.asarray(inp["L0_attn_b_proj_b"], f).reshape(4, 32).T)
    out["bgt"] = np.asarray(inp["L0_ffn_gate_bias"], f).reshape(8, 1)
    out["bcv"] = np.ascontiguousarray(
        np.asarray(inp["L0_attn_conv_b"], f).reshape(8, 128).T)
    out["wcv"] = np.ascontiguousarray(
        np.asarray(inp["L0_attn_conv_w"], f)[:, 0, :].reshape(8, 128, KCV)
        .transpose(1, 0, 2))
    out["alog"] = np.ascontiguousarray(
        np.asarray(inp["L0_attn_a_log"], f).reshape(8, 128, NST).transpose(1, 0, 2))
    return out


def build():
    nc = bass.Bass("TRN2", target_bir_lowering=False)

    x_h = nc.dram_tensor("x", [2, 2048, D], F32, kind="ExternalInput").ap()
    ws_h = nc.dram_tensor("wstream", [128, N_PIECE, 8, 128], F32,
                          kind="ExternalInput").ap()
    wbp_h = nc.dram_tensor("wbp", [128, 8, 4, 32], F32, kind="ExternalInput").ap()
    wgt_h = nc.dram_tensor("wgt", [128, 8, 8], F32, kind="ExternalInput").ap()
    csel_h = nc.dram_tensor("csel", [8, 32, 16, 128], F32, kind="ExternalInput").ap()
    selc8_h = nc.dram_tensor("selc8", [32, 8, 128], F32, kind="ExternalInput").ap()
    ident_h = nc.dram_tensor("ident", [128, 128], F32, kind="ExternalInput").ap()
    bip_h = nc.dram_tensor("bip", [128, 16], F32, kind="ExternalInput").ap()
    bbp_h = nc.dram_tensor("bbp", [32, 4], F32, kind="ExternalInput").ap()
    bgt_h = nc.dram_tensor("bgt", [8, 1], F32, kind="ExternalInput").ap()
    bcv_h = nc.dram_tensor("bcv", [128, 8], F32, kind="ExternalInput").ap()
    wcv_h = nc.dram_tensor("wcv", [128, 8, KCV], F32, kind="ExternalInput").ap()
    alog_h = nc.dram_tensor("alog", [128, 8, NST], F32, kind="ExternalInput").ap()

    out_h = nc.dram_tensor("out", [2, 2048, D], F32, kind="ExternalOutput").ap()
    wsb_h = nc.dram_tensor("wsbf", [128, N_PIECE, 8, 128], BF16, kind="Internal").ap()
    csb_h = nc.dram_tensor("cselbf", [8, 32, 16, 128], BF16, kind="Internal").ap()

    with ExitStack() as ctx:
        en = ctx.enter_context

        def T(name, shape, dt):
            return en(nc.sbuf_tensor(name, shape, dt))

        env = {}
        ring = env["ring"] = T("ring", [128, RING, 8, 128], BF16)
        x_tok = env["x_tok"] = T("x_tok", [128, 4, D], F32)
        xnp = env["xnp"] = T("xnp", [128, 2, 128], F32)
        xn_fm = env["xn_fm"] = T("xn_fm", [128, 8, TC], BF16)
        u_pad = env["u_pad"] = T("u_pad", [128, 8, 515], BF16)
        v_s = env["v_s"] = T("v_s", [128, 8, TC], BF16)
        dlt = env["dlt"] = T("dlt", [128, 8, TC], BF16)
        sp_t = env["sp_t"] = T("sp_t", [128, 2, TC], F32)
        uc = env["uc"] = T("uc", [128, 2, TC], F32)
        du = env["du"] = T("du", [128, 8, TC], BF16)
        b_sb = env["b_sb"] = T("b_sb", [32, 4, TC], BF16)
        dA = env["dA"] = T("dA", [128, 4, SCP], F32)
        dBu = env["dBu"] = T("dBu", [128, 2, SCP], BF16)
        hsc0 = env["hsc0"] = T("hsc0", [128, SCP], BF16)
        stt_t = env["stt_t"] = T("state", [128, 128], F32)
        ccr = env["ccr"] = T("ccr", [128, 8, 3], BF16)
        ysc = env["ysc"] = T("ysc", [128, 2, TC], F32)
        acc = env["acc"] = T("acc", [128, 8, TC], F32)
        racc = env["racc"] = T("racc", [128, 8, TC], F32)
        accb = env["accb"] = T("accb", [128, 2, 128], F32)
        a_s = env["a_s"] = T("a_s", [128, 2, TC], BF16)
        b_bf = env["b_bf"] = T("b_bf", [128, 2, TC], BF16)
        smid = env["smid"] = T("smid", [128, 8, TC], BF16)
        hid = env["hid"] = T("hid", [128, 8, TC], BF16)
        gb = env["gb"] = T("gb", [128, 8, TC], BF16)
        hg = env["hg"] = T("hg", [128, 2, 8, TC], BF16)
        hsc1 = env["hsc1"] = T("hsc1", [128, SCP], BF16)
        lg = env["lg"] = T("lg", [32, TC], F32)
        wdf = env["wdf"] = T("wdf", [32, TC], F32)
        gt_tok = env["gt_tok"] = T("gt_tok", [128, 4, 8], F32)
        gpr = env["gpr"] = T("gpr", [128, 4, 8], F32)
        gtmp = env["gtmp"] = T("gtmp", [128, 4, 8], F32)
        gcols = env["gcols"] = T("gcols", [128, 4, 4], F32)
        stats = env["stats"] = T("stats", [128, 24], F32)
        csel_sb = env["csel_sb"] = T("csel_sb", [32, 2, 16, 128], BF16)
        wbp_sb = env["wbp_sb"] = T("wbp_sb", [128, 8, 4, 32], BF16)
        wgt_sb = env["wgt_sb"] = T("wgt_sb", [128, 8, 8], BF16)
        selc8_sb = env["selc8_sb"] = T("selc8_sb", [32, 8, 128], F32)
        ident3 = env["ident3"] = T("ident3", [128, 128], F32)
        bip_sb = env["bip_sb"] = T("bip_sb", [128, 16], F32)
        bbp_sb = env["bbp_sb"] = T("bbp_sb", [32, 4], F32)
        bgt_sb = env["bgt_sb"] = T("bgt_sb", [8, 1], F32)
        bcv_sb = env["bcv_sb"] = T("bcv_sb", [128, 8], F32)
        wcv_sb = env["wcv_sb"] = T("wcv_sb", [128, 8, KCV], F32)
        A_sb = env["A_sb"] = T("A_sb", [128, 8, NST], F32)

        pAB = env["pAB"] = en(nc.psum_tensor("pAB", [128, 2, TC], F32))
        pFF = env["pFF"] = en(nc.psum_tensor("pFF", [128, 2, TC], F32))
        pBCb = env["pBCb"] = en(nc.psum_tensor("pBCb", [128, 2, TC], F32))
        pEG = env["pEG"] = en(nc.psum_tensor("pEG", [128, 2, TC], F32))
        env["csb_h"] = csb_h

        sems = {}
        for nm in ["sx", "sring0", "sring1", "sring2", "sring3", "swf", "sout", "spro", "spro_s", "spro_c",
                   "smsc",
                   "st_xp", "sa_xe", "sv_ln", "sv_st", "sa_st",
                   "sa_g", "st_bp", "sa_bp",
                   "sa_cv", "sv_cv", "sp_cv",
                   "st_gt", "sv_gt", "sa_gt",
                   "st_bc", "sv_bc", "sa_da", "sv_sc", "sp_y",
                   "sa_f", "sa_b", "sp_f", "sv_f", "st_gb", "sa_gb",
                   "sv_em", "sv_hg",
                   "st_op", "sv_oc", "sv_oa", "sch_a", "sch_v", "s_cs"]:
            sems[nm] = en(nc.semaphore(nm))
        S = type("SS", (), sems)

        pull_next = [0]
        chain_a = [0]
        chain_v = [0]

        def a_link(ins):
            chain_a[0] += 1
            ins.then_inc(S.sch_a, 1)

        def a_wait(eng):
            eng.wait_ge(S.sch_a, chain_a[0])

        def v_link(ins):
            chain_v[0] += 1
            ins.then_inc(S.sch_v, 1)

        def v_wait(eng):
            eng.wait_ge(S.sch_v, chain_v[0])

        def emit_pulls(s, upto_consumed):
            while pull_next[0] < NPULL * NCH:
                gp = pull_next[0]
                gate = PULL * (gp + 1) - RING
                if gate > upto_consumed:
                    break
                c_, j_ = divmod(gp, NPULL)
                base_piece = j_ * PULL
                if gp < NPULL:
                    s.wait_ge(S.spro_c, 16 * 9)
                if gate > 0:
                    s.wait_ge(S.swf, gate)
                sr = (S.sring0, S.sring1, S.sring2, S.sring3)[gp % 4]
                if gp > 3:
                    s.wait_ge(sr, 16 * (gp // 4))
                s.dma_start(
                    ring[:, base_piece % RING:base_piece % RING + PULL, :, :],
                    wsb_h[:, base_piece:base_piece + PULL, :, :],
                ).then_inc(sr, 16)
                pull_next[0] += 1

        def ring_piece(p):
            return ring[:, POS[p] % RING, :, :]

        def wait_piece(t, c, p):
            gp = c * NPULL + POS[p] // PULL
            t.wait_ge((S.sring0, S.sring1, S.sring2, S.sring3)[gp % 4],
                      16 * (gp // 4 + 1))

        env.update(ring_piece=ring_piece, wait_piece=wait_piece,
                   emit_pulls=emit_pulls, v_link=v_link, v_wait=v_wait,
                   a_link=a_link, a_wait=a_wait)

        with nc.Block("prog") as blk:
            @blk.gpsimd
            def _(g):
                for j in range(8):
                    g.dma_start(wsb_h[:, 16 * j:16 * (j + 1), :, :],
                                ws_h[:, 16 * j:16 * (j + 1), :, :]
                                ).then_inc(S.spro_c, 16)
                g.dma_start(csb_h, csel_h).then_inc(S.spro_c, 16)
                g.dma_start(wbp_sb[:], wbp_h).then_inc(S.spro, 16)
                g.dma_start(wgt_sb[:], wgt_h).then_inc(S.spro, 16)

            @blk.sync
            def _(s):
                s.dma_start(selc8_sb[:], selc8_h).then_inc(S.spro_s, 16)
                s.dma_start(ident3[:], ident_h).then_inc(S.spro_s, 16)
                s.dma_start(bip_sb[:], bip_h).then_inc(S.spro_s, 16)
                s.dma_start(bbp_sb[:], bbp_h).then_inc(S.spro_s, 16)
                s.dma_start(bgt_sb[:], bgt_h).then_inc(S.spro_s, 16)
                s.dma_start(bcv_sb[:], bcv_h).then_inc(S.spro_s, 16)
                s.dma_start(wcv_sb[:], wcv_h).then_inc(S.spro_s, 16)
                s.dma_start(A_sb[:], alog_h).then_inc(S.spro_s, 16)
                for h in range(2):
                    s.dma_start(
                        x_tok[:, 2 * h:2 * h + 2, :],
                        x_h[0:1, 256 * h:256 * h + 256, :].squeeze(0)
                        .rearrange("(q p) d -> p q d", p=128),
                    ).then_inc(S.sx, 16)
                emit_pulls(s, 0)

            @blk.scalar
            def _(sc):
                sc.wait_ge(S.spro, 16 * 2)
                sc.wait_ge(S.spro_s, 16 * 8)
                sc.activation(A_sb[:], A_sb[:], AF.Exp).then_inc(S.smsc, 1)
                sc.wait_ge(S.smsc, 1)
                sc.mul(A_sb[:], A_sb[:], -1.0).then_inc(S.smsc, 1)

            for c in range(NCH):
                b, tseg = divmod(c, 4)
                t0 = tseg * TC

                @blk.sync
                def _(s, c=c, b=b, t0=t0):
                    if c > 0:
                        for h in range(2):
                            s.wait_ge(S.sout, 32 * (c - 1) + 16 * (h + 1))
                            s.dma_start(
                                x_tok[:, 2 * h:2 * h + 2, :],
                                x_h[b:b + 1, t0 + 256 * h:t0 + 256 * h + 256, :]
                                .squeeze(0).rearrange("(q p) d -> p q d", p=128),
                            ).then_inc(S.sx, 16)
                    emit_pulls(s, 128 * c)

                @blk.vector
                def _(v, c=c):
                    v.wait_ge(S.sx, 32 * (c + 1))
                    for i in range(4):
                        if i >= 1:
                            v.wait_ge(S.sv_st, C_SVST * c + i)
                        v.tensor_scalar(uc[:].rearrange("p a b -> p (a b)"),
                                        x_tok[:, i, :], 1.0, 0.0,
                                        ALU.mult, ALU.add,
                                        accum_out=stats[:, i:i + 1]
                                        ).then_inc(S.sv_st, 1)
                    v.wait_ge(S.sv_st, C_SVST * c + 4)
                    v.wait_ge(S.sa_st, C_SAST * c + 4)
                    v.tensor_scalar(stats[:, 8:12], stats[:, 0:4], 1.0 / D,
                                    None, ALU.mult).then_inc(S.sv_st, 1)
                    v.tensor_scalar(stats[:, 12:16], stats[:, 4:8], 1.0 / D,
                                    1e-5, ALU.mult, ALU.add).then_inc(S.sv_st, 1)
                    v.wait_ge(S.sv_st, C_SVST * c + 6)
                    v.tensor_tensor(stats[:, 16:20], stats[:, 8:12],
                                    stats[:, 8:12], ALU.mult).then_inc(S.sv_st, 1)
                    v.wait_ge(S.sv_st, C_SVST * c + 7)
                    v.tensor_tensor(stats[:, 12:16], stats[:, 12:16],
                                    stats[:, 16:20], ALU.subtract
                                    ).then_inc(S.sv_st, 1)
                    v.wait_ge(S.sa_st, C_SAST * c + 6)
                    for j in range(32):
                        i, gg = divmod(j, 8)
                        if j >= 2:
                            v.wait_ge(S.st_xp, C_STXP * c + j - 1)
                        elif c > 0:
                            v.wait_ge(S.st_xp, C_STXP * c)
                        v.tensor_scalar(
                            xnp[:, j % 2, :],
                            x_tok[:, i, gg * 128:(gg + 1) * 128],
                            stats[:, 8 + i:9 + i], stats[:, 12 + i:13 + i],
                            ALU.subtract, ALU.mult).then_inc(S.sv_ln, 1)

                @blk.scalar
                def _(sc, c=c):
                    sc.wait_ge(S.sx, 32 * (c + 1))
                    for i in range(4):
                        if i >= 1:
                            sc.wait_ge(S.sa_st, C_SAST * c + i)
                        sc.activation(sp_t[:].rearrange("p a b -> p (a b)"),
                                      x_tok[:, i, :], AF.Square,
                                      accum_out=stats[:, 4 + i:5 + i]
                                      ).then_inc(S.sa_st, 1)
                    sc.wait_ge(S.sv_st, C_SVST * c + 8)
                    sc.activation(stats[:, 20:24], stats[:, 12:16], AF.Ln
                                  ).then_inc(S.sa_st, 1)
                    sc.wait_ge(S.sa_st, C_SAST * c + 5)
                    sc.activation(stats[:, 12:16], stats[:, 20:24], AF.Exp,
                                  scale=-0.5).then_inc(S.sa_st, 1)
                    for j in range(32):
                        i, gg = divmod(j, 8)
                        sc.wait_ge(S.st_xp, C_STXP * c + j + 1)
                        sc.activation(xn_fm[:, gg, i * 128:(i + 1) * 128],
                                      pBCb[:, j % 2, 0:128], AF.Copy
                                      ).then_inc(S.sa_xe, 1)

                @blk.tensor
                def _(t, c=c):
                    if c == 0:
                        t.wait_ge(S.spro, 16 * 2)
                        t.wait_ge(S.spro_s, 16 * 8)
                    for j in range(32):
                        t.wait_ge(S.sv_ln, C_SVLN * c + j + 1)
                        if j >= 2:
                            t.wait_ge(S.sa_xe, C_SAXE * c + j - 1)
                        elif c > 0:
                            t.wait_ge(S.sv_oa, C_SVOA * c)
                        t.transpose(pBCb[:, j % 2, 0:128], xnp[:, j % 2, :],
                                    ident3[:]).then_inc(S.st_xp, 1)

                @blk.tensor
                def _(t, c=c):
                    t.wait_ge(S.sa_xe, C_SAXE * c + 32)
                    if c > 0:
                        t.wait_ge(S.sv_f, C_SVF * c)
                        t.wait_ge(S.sa_f, C_SAF * c)
                        t.wait_ge(S.sa_b, C_SAB * c)
                    for i in range(8):
                        pp = pAB if i % 2 == 0 else pFF
                        if i >= 2:
                            t.wait_ge(S.sa_g, C_SAG * c + i - 1)
                        for half in range(2):
                            mb = 2 * i + half
                            wait_piece(t, c, mb)
                            for kc in range(NK):
                                ins = t.matmul(pp[:, half, :],
                                               ring_piece(mb)[:, kc, :],
                                               xn_fm[:, kc, :],
                                               start=(kc == 0),
                                               stop=(kc == NK - 1))
                            ins.then_inc(S.swf, 1)
                    for i in range(4):
                        gi = 8 + i
                        pp = pAB if gi % 2 == 0 else pFF
                        t.wait_ge(S.sa_g, C_SAG * c + gi - 1)
                        for half in range(2):
                            p_idx = 16 + 2 * i + half
                            wait_piece(t, c, p_idx)
                            for kc in range(NK):
                                ins = t.matmul(pp[:, half, :],
                                               ring_piece(p_idx)[:, kc, :],
                                               u_pad[:, kc, 3:3 + TC],
                                               start=(kc == 0),
                                               stop=(kc == NK - 1))
                            ins.then_inc(S.swf, 1)
                    base_bp = C_STBP * c
                    if c > 0:
                        t.wait_ge(S.sv_em, C_SVEM * c)
                        t.wait_ge(S.sa_gb, C_SAGB * c)
                    for s_ in range(4):
                        if s_ >= 2:
                            t.wait_ge(S.sa_bp, base_bp + s_ - 1)
                        for kc in range(NK):
                            ins = t.matmul(pEG[0:32, s_ % 2, :],
                                           wbp_sb[:, kc, s_, :],
                                           u_pad[:, kc, 3:3 + TC],
                                           start=(kc == 0), stop=(kc == NK - 1))
                        ins.then_inc(S.st_bp, 1)
                    t.wait_ge(S.sa_bp, base_bp + 3)
                    for kc in range(NK):
                        ins = t.matmul(pEG[0:8, 0, :], wgt_sb[:, kc, :],
                                       xn_fm[:, kc, :],
                                       start=(kc == 0), stop=(kc == NK - 1))
                    ins.then_inc(S.st_bp, 1)

                @blk.scalar
                def _(sc, c=c):
                    def conv_uc(gg):
                        if gg == 0:
                            sc.wait_ge(S.sv_cv, C_SVCV * c + 1)
                            if c > 0:
                                sc.wait_ge(S.sv_bc, C_SVBC * c)
                        sc.activation(uc[:, gg % 2, :], u_pad[:, gg, 3:3 + TC],
                                      AF.Identity, scale=wcv_sb[:, gg, 3:4]
                                      ).then_inc(S.sa_cv, 1)

                    def conv_silu(gg):
                        sc.wait_ge(S.sv_cv, C_SVCV * c + 2 * gg + 3)
                        sc.activation(du[:, gg, :], uc[:, gg % 2, :], AF.Silu,
                                      bias=bcv_sb[:, gg:gg + 1]).then_inc(S.sa_cv, 1)

                    for i in range(8):
                        pp = pAB if i % 2 == 0 else pFF
                        sc.wait_ge(S.swf, 128 * c + 2 * i + 2)
                        if i == 0 and c > 0:
                            sc.wait_ge(S.sv_cv, C_SVCV * c)
                        if i == 4 and c > 0:
                            sc.wait_ge(S.sp_y, C_SPY * c)
                        for half in range(2):
                            mb = 2 * i + half
                            if mb < 8:
                                ins = sc.activation(u_pad[:, mb, 3:515],
                                                    pp[:, half, :], AF.Identity,
                                                    bias=bip_sb[:, mb:mb + 1])
                            else:
                                ins = sc.activation(v_s[:, mb - 8, :],
                                                    pp[:, half, :], AF.Silu,
                                                    bias=bip_sb[:, mb:mb + 1])
                        ins.then_inc(S.sa_g, 1)
                        if i >= 1:
                            conv_silu(i - 1)
                        conv_uc(i)
                    conv_silu(7)
                    for i in range(4):
                        gi = 8 + i
                        pp = pAB if gi % 2 == 0 else pFF
                        sc.wait_ge(S.swf, 128 * c + 16 + 2 * i + 2)
                        if i >= 1:
                            sc.wait_ge(S.sa_g, C_SAG * c + 8 + i)
                        if i == 0 and c > 0:
                            sc.wait_ge(S.sp_cv, C_SPCV * c)
                        a_link(sc.activation(sp_t[:], pp[:, :, :], AF.Exp))
                        a_wait(sc)
                        sc.activation(dlt[:, 2 * i:2 * i + 2, :], sp_t[:],
                                      AF.Ln, bias=1.0).then_inc(S.sa_g, 1)
                    base_bp = C_STBP * c
                    for s_ in range(4):
                        sc.wait_ge(S.st_bp, base_bp + s_ + 1)
                        sc.activation(b_sb[:, s_, :], pEG[0:32, s_ % 2, :],
                                      AF.Identity, bias=bbp_sb[:, s_:s_ + 1]
                                      ).then_inc(S.sa_bp, 1)
                    sc.wait_ge(S.st_bp, base_bp + 5)
                    sc.activation(lg[0:8, :], pEG[0:8, 0, :], AF.Identity,
                                  bias=bgt_sb[:]).then_inc(S.sa_bp, 1)

                @blk.sync
                def _(s, c=c):
                    emit_pulls(s, 128 * c + 24)

                emit_main(nc, blk, S, c, tseg, env)

                @blk.vector
                def _(v, c=c):
                    for j in range(32):
                        i, gg = divmod(j, 8)
                        if j >= 2:
                            v.wait_ge(S.st_op, C_STOP * c + j - 1)
                        elif c > 0:
                            v.wait_ge(S.st_op, C_STOP * c)
                        v.tensor_tensor(accb[:, j % 2, :],
                                        acc[:, gg, i * 128:(i + 1) * 128],
                                        racc[:, gg, i * 128:(i + 1) * 128],
                                        ALU.add).then_inc(S.sv_oc, 1)
                        v.wait_ge(S.st_op, C_STOP * c + j + 1)
                        v.tensor_tensor(x_tok[:, i, gg * 128:(gg + 1) * 128],
                                        x_tok[:, i, gg * 128:(gg + 1) * 128],
                                        pBCb[:, j % 2, 0:128], ALU.add
                                        ).then_inc(S.sv_oa, 1)

                @blk.tensor
                def _(t, c=c):
                    for j in range(32):
                        t.wait_ge(S.sv_oc, C_SVOC * c + j + 1)
                        if j >= 2:
                            t.wait_ge(S.sv_oa, C_SVOA * c + j - 1)
                        elif j == 0:
                            t.wait_ge(S.sv_bc, C_SVBC * c + 128)
                        t.transpose(pBCb[:, j % 2, 0:128], accb[:, j % 2, :],
                                    ident3[:]).then_inc(S.st_op, 1)

                @blk.sync
                def _(s, c=c, b=b, t0=t0):
                    for h in range(2):
                        s.wait_ge(S.sv_oa, C_SVOA * c + 16 * (h + 1))
                        s.dma_start(
                            out_h[b:b + 1, t0 + 256 * h:t0 + 256 * h + 256, :]
                            .squeeze(0).rearrange("(q p) d -> p q d", p=128),
                            x_tok[:, 2 * h:2 * h + 2, :],
                        ).then_inc(S.sout, 16)

            @blk.sync
            def _(s):
                s.wait_ge(S.sout, 32 * NCH)

    return nc


def emit_main(nc, blk, S, c, tseg, env):
    ring_piece = env["ring_piece"]; wait_piece = env["wait_piece"]
    emit_pulls = env["emit_pulls"]
    v_link = env["v_link"]; v_wait = env["v_wait"]
    a_link = env["a_link"]; a_wait = env["a_wait"]
    u_pad = env["u_pad"]; v_s = env["v_s"]; dlt = env["dlt"]; uc = env["uc"]
    du = env["du"]; b_sb = env["b_sb"]; dA = env["dA"]; dBu = env["dBu"]
    hsc0 = env["hsc0"]; hsc1 = env["hsc1"]
    stt_t = env["stt_t"]; ccr = env["ccr"]; ysc = env["ysc"]
    acc = env["acc"]; racc = env["racc"]; a_s = env["a_s"]; b_bf = env["b_bf"]
    smid = env["smid"]; hid = env["hid"]; gb = env["gb"]; hg = env["hg"]
    lg = env["lg"]; wdf = env["wdf"]; gt_tok = env["gt_tok"]; gpr = env["gpr"]
    gtmp = env["gtmp"]; gcols = env["gcols"]
    csel_sb = env["csel_sb"]; selc8_sb = env["selc8_sb"]; ident3 = env["ident3"]
    wcv_sb = env["wcv_sb"]; bcv_sb = env["bcv_sb"]; A_sb = env["A_sb"]
    xn_fm = env["xn_fm"]
    pAB = env["pAB"]; pBCb = env["pBCb"]; pEG = env["pEG"]; pFF = env["pFF"]
    csb_h = env["csb_h"]

    @blk.sync
    def _(s):
        emit_pulls(s, 128 * c + 128)

    @blk.scalar
    def _(sc):
        def csel_dma(gg):
            gidx = 8 * c + gg
            if gg >= 2:
                sc.wait_ge(S.st_bc, C_STBC * c + 16 * (gg - 1))
            if gidx > 0:
                sc.wait_ge(S.s_cs, 16 * gidx)
            sc.dma_start(csel_sb[:, gg % 2, :, :], csb_h[gg]).then_inc(S.s_cs, 16)

        csel_dma(0)
        csel_dma(1)
        for i in range(4):
            sc.wait_ge(S.sv_gt, C_SVGT * c + 12 * i + 3)
            sc.activation(gpr[:, i, :], gt_tok[:, i, :], AF.Exp,
                          bias=gcols[:, i, 0:1]).then_inc(S.sa_gt, 1)
        ai = 0
        for j in range(128):
            gg, n = divmod(j, NST)
            if n == 0 and gg >= 2:
                csel_dma(gg)
            if j >= 4:
                sc.wait_ge(S.sv_sc, C_SVSC * c + j - 3)
            elif c > 0:
                sc.wait_ge(S.sv_sc, C_SVSC * c)
            sc.activation(dA[:, j % 4, 0:TC], dlt[:, gg, :], AF.Exp,
                          scale=A_sb[:, gg, n:n + 1]).then_inc(S.sa_da, 1)
            while ai < len(A_EVS) and A_EVS[ai][0] <= j:
                kind = A_EVS[ai][1]
                ai += 1
                if kind[0] == "af":
                    jj = kind[1]
                    pp = pAB if jj % 2 == 0 else pFF
                    sc.wait_ge(S.swf, 128 * c + SWF_AFTER[("si", jj, 0)])
                    if jj >= 2:
                        sc.wait_ge(S.sp_f, C_SPF * c + jj - 1)
                    elif c > 0:
                        sc.wait_ge(S.sp_f, C_SPF * c)
                    sc.activation(a_s[:, jj % 2, :], pp[:, 0, :], AF.Silu
                                  ).then_inc(S.sa_f, 1)
                    sc.wait_ge(S.swf, 128 * c + SWF_AFTER[("si", jj, 1)])
                    sc.activation(b_bf[:, jj % 2, :], pp[:, 1, :], AF.Copy
                                  ).then_inc(S.sa_b, 1)
                elif kind[0] == "wdf":
                    i = kind[1]
                    sc.wait_ge(S.st_gt, C_STGT * c + 4 + i + 1)
                    sc.activation(wdf[0:8, i * 128:(i + 1) * 128],
                                  pEG[0:8, i % 2, 8:136], AF.Copy
                                  ).then_inc(S.sa_gt, 1)
                else:
                    e = kind[1]
                    sc.wait_ge(S.st_gb, C_STGB * c + e + 1)
                    if e == 0 and c > 0:
                        sc.wait_ge(S.sv_hg, C_SVHG * c)
                    sc.activation(gb[:, e, :], pEG[:, e % 2, :], AF.Copy
                                  ).then_inc(S.sa_gb, 1)
        assert ai == len(A_EVS)

    @blk.vector
    def _(v):
        if tseg == 0:
            v.memset(u_pad[:, :, 0:3], 0.0).then_inc(S.sv_cv, 1)
        else:
            v.tensor_copy(u_pad[:, :, 0:3], ccr[:]).then_inc(S.sv_cv, 1)
        for gg in range(8):
            v.wait_ge(S.sa_g, C_SAG * c + gg // 2 + 1)
            v.tensor_copy(ccr[:, gg, :], u_pad[:, gg, 512:515]
                          ).then_inc(S.sv_cv, 1)
            v.wait_ge(S.sa_cv, C_SACV * c + 2 * gg + 1)
            for k in range(3):
                if k > 0:
                    v_wait(v)
                ins = v.scalar_tensor_tensor(
                    uc[:, gg % 2, :], u_pad[:, gg, k:k + TC],
                    wcv_sb[:, gg, k:k + 1], uc[:, gg % 2, :],
                    ALU.mult, ALU.add)
                if k < 2:
                    v_link(ins)
            ins.then_inc(S.sv_cv, 1)
        k = C_SVGT * c
        for i in range(4):
            v.wait_ge(S.st_gt, C_STGT * c + i + 1)
            v.tensor_copy(gt_tok[:, i, :], pEG[:, i % 2, 0:8]
                          ).then_inc(S.sv_gt, 1); k += 1
            v.wait_ge(S.sv_gt, k)
            v.tensor_reduce(gcols[:, i, 0:1], gt_tok[:, i, :], AX.X,
                            ALU.max).then_inc(S.sv_gt, 1); k += 1
            v.wait_ge(S.sv_gt, k)
            v.tensor_scalar(gcols[:, i, 0:1], gcols[:, i, 0:1], -1.0,
                            None, ALU.mult).then_inc(S.sv_gt, 1); k += 1
            v.wait_ge(S.sv_gt, k)
            v.wait_ge(S.sa_gt, C_SAGT * c + i + 1)
            v.tensor_reduce(gcols[:, i, 1:2], gpr[:, i, :], AX.X,
                            ALU.add).then_inc(S.sv_gt, 1); k += 1
            v.wait_ge(S.sv_gt, k)
            v.reciprocal(gcols[:, i, 2:3], gcols[:, i, 1:2]).then_inc(S.sv_gt, 1)
            k += 1
            v.wait_ge(S.sv_gt, k)
            v.tensor_scalar(gpr[:, i, :], gpr[:, i, :], gcols[:, i, 2:3],
                            None, ALU.mult).then_inc(S.sv_gt, 1); k += 1
            v.wait_ge(S.sv_gt, k)
            v.tensor_reduce(gcols[:, i, 0:1], gpr[:, i, :], AX.X,
                            ALU.max).then_inc(S.sv_gt, 1); k += 1
            v.wait_ge(S.sv_gt, k)
            v.tensor_scalar(gtmp[:, i, :], gpr[:, i, :], gcols[:, i, 0:1],
                            -1e30, ALU.is_ge, ALU.mult).then_inc(S.sv_gt, 1)
            k += 1
            v.wait_ge(S.sv_gt, k)
            v.tensor_tensor(gtmp[:, i, :], gpr[:, i, :], gtmp[:, i, :],
                            ALU.add).then_inc(S.sv_gt, 1); k += 1
            v.wait_ge(S.sv_gt, k)
            v.tensor_reduce(gcols[:, i, 3:4], gtmp[:, i, :], AX.X,
                            ALU.max).then_inc(S.sv_gt, 1); k += 1
            v.wait_ge(S.sv_gt, k)
            v.tensor_scalar(gtmp[:, i, :], gpr[:, i, :], gcols[:, i, 3:4],
                            None, ALU.is_ge).then_inc(S.sv_gt, 1); k += 1
            v.wait_ge(S.sv_gt, k)
            v.tensor_tensor(gpr[:, i, :], gpr[:, i, :], gtmp[:, i, :],
                            ALU.mult).then_inc(S.sv_gt, 1); k += 1
            v.wait_ge(S.sv_gt, k)

        def v_insert(kind):
            if kind[0] == "hg":
                ep = kind[1]
                if ep >= 1:
                    v.wait_ge(S.swf, 128 * c + SWF_AFTER[("exp", ep - 1, 7)])
                elif c > 0:
                    v.wait_ge(S.swf, 128 * c)
                v.wait_ge(S.sa_gb, C_SAGB * c + 2 * ep + 2)
                v.wait_ge(S.sp_f, C_SPF * c + 16)
                for ii, e in enumerate((2 * ep, 2 * ep + 1)):
                    for kc in range(NK):
                        ins = v.tensor_tensor(hg[:, ii, kc, :], hid[:, kc, :],
                                              gb[:, e, :], ALU.mult)
                ins.then_inc(S.sv_hg, 1)
            elif kind[0] == "soadd":
                i = kind[1]
                pp = pAB if (16 + i) % 2 == 0 else pFF
                v.wait_ge(S.swf, 128 * c + SWF_AFTER[("so", i, 1)])
                ja = 32 * i + 31
                v.wait_ge(S.sp_y, C_SPY * c + ja + ja // 16 + 2)
                v.tensor_tensor(acc[:, 2 * i:2 * i + 2, :],
                                acc[:, 2 * i:2 * i + 2, :], pp[:, :, :],
                                ALU.add).then_inc(S.sv_f, 1)
            else:
                ep, m = kind[1], kind[2]
                q = 8 * ep + m
                v.wait_ge(S.swf, 128 * c + SWF_AFTER[("exp", ep, m)])
                if ep == 0:
                    ins = v.tensor_copy(racc[:, m, :], pEG[:, m % 2, :])
                else:
                    ins = v.tensor_tensor(racc[:, m, :], racc[:, m, :],
                                          pEG[:, m % 2, :], ALU.add)
                ins.then_inc(S.sv_em, 1)

        vi = 0
        for j in range(128):
            gg, n = divmod(j, NST)
            while vi < len(V_INS) and V_INS[vi][0] <= j:
                v_insert(V_INS[vi][1]); vi += 1
            v.wait_ge(S.st_bc, C_STBC * c + j + 1)
            if n == 0:
                v.wait_ge(S.sp_cv, C_SPCV * c + gg + 1)
            if j >= 2:
                v.wait_ge(S.sp_y, C_SPY * c + (j - 2) + (j - 2) // 16 + 1)
            v.tensor_tensor(dBu[:, j % 2, 0:TC], du[:, gg, :],
                            pBCb[:, j % 2, :], ALU.mult).then_inc(S.sv_bc, 1)
            v.wait_ge(S.sa_da, C_SADA * c + j + 1)
            v.wait_ge(S.sv_bc, C_SVBC * c + j + 1)
            init = 0.0 if tseg == 0 else stt_t[:, j:j + 1]
            hs = hsc0 if j % 2 == 0 else hsc1
            v.tensor_tensor_scan(hs[:, 0:TC], dA[:, j % 4, 0:TC],
                                 dBu[:, j % 2, 0:TC], init,
                                 ALU.mult, ALU.add).then_inc(S.sv_sc, 1)
        while vi < len(V_INS):
            v_insert(V_INS[vi][1]); vi += 1

    @blk.gpsimd
    def _(gp):
        for gg in range(8):
            gp.wait_ge(S.sa_cv, C_SACV * c + 2 * gg + 2)
            gp.wait_ge(S.sa_g, C_SAG * c + 8 + gg // 2 + 1)
            gp.tensor_tensor(du[:, gg, :], du[:, gg, :], dlt[:, gg, :],
                             ALU.mult).then_inc(S.sp_cv, 1)
        pi = 0
        for j in range(128):
            gg, n = divmod(j, NST)
            while pi < len(P_INS) and P_INS[pi][0] <= j:
                jj = P_INS[pi][1]
                pi += 1
                gp.wait_ge(S.sa_f, C_SAF * c + jj + 1)
                gp.wait_ge(S.sa_b, C_SAB * c + jj + 1)
                if jj == 8 and c > 0:
                    gp.wait_ge(S.sv_hg, C_SVHG * c)
                dst = smid[:, jj, :] if jj < 8 else hid[:, jj - 8, :]
                gp.tensor_tensor(dst, a_s[:, jj % 2, :], b_bf[:, jj % 2, :],
                                 ALU.mult).then_inc(S.sp_f, 1)
            gp.wait_ge(S.sv_sc, C_SVSC * c + j + 1)
            gp.wait_ge(S.sp_y, C_SPY * c + j + j // 16)
            hs = hsc0 if j % 2 == 0 else hsc1
            if tseg < 3:
                gp.tensor_copy(stt_t[:, j:j + 1], hs[:, TC - 1:TC])
            if n == 0:
                ins = gp.tensor_copy(ysc[:, gg % 2, :], hs[:, 0:TC])
            else:
                ins = gp.tensor_tensor(ysc[:, gg % 2, :], hs[:, 0:TC],
                                       ysc[:, gg % 2, :], ALU.add)
            ins.then_inc(S.sp_y, 1)
            if n == NST - 1:
                gp.wait_ge(S.sp_y, C_SPY * c + j + j // 16 + 1)
                gp.tensor_tensor(acc[:, gg, :], ysc[:, gg % 2, :],
                                 v_s[:, gg, :], ALU.mult).then_inc(S.sp_y, 1)
        assert pi == len(P_INS)

    @blk.tensor
    def _(t):
        t.wait_ge(S.sa_bp, C_SABP * c + 5)
        for i in range(4):
            if i >= 2:
                t.wait_ge(S.sv_gt, C_SVGT * c + 12 * (i - 2) + 1)
            t.transpose(pEG[:, i % 2, 0:8], lg[0:8, i * 128:(i + 1) * 128],
                        ident3[0:8, 0:8]).then_inc(S.st_gt, 1)

        def emit_item(item):
            if item[0] == "si":
                jj, half = item[1], item[2]
                pp = pAB if jj % 2 == 0 else pFF
                if jj >= 2 and half == 0:
                    t.wait_ge(S.sa_f, C_SAF * c + jj - 1)
                    t.wait_ge(S.sa_b, C_SAB * c + jj - 1)
                elif jj < 2 and half == 0:
                    t.wait_ge(S.sa_g, C_SAG * c + 12)
                p_idx = (24 + jj * 2 + half) if jj < 8 else (40 + (jj - 8) * 2 + half)
                wait_piece(t, c, p_idx)
                for kc in range(NK):
                    ins = t.matmul(pp[:, half, :], ring_piece(p_idx)[:, kc, :],
                                   xn_fm[:, kc, :],
                                   start=(kc == 0), stop=(kc == NK - 1))
                ins.then_inc(S.swf, 1)
            elif item[0] == "so":
                i, half = item[1], item[2]
                jj = 16 + i
                pp = pAB if jj % 2 == 0 else pFF
                if half == 0:
                    t.wait_ge(S.sa_f, C_SAF * c + min(jj - 1, 16))
                    t.wait_ge(S.sa_b, C_SAB * c + min(jj - 1, 16))
                    if i >= 2:
                        t.wait_ge(S.sv_f, C_SVF * c + i - 1)
                    t.wait_ge(S.sp_f, C_SPF * c + 8)
                p_idx = 56 + 2 * i + half
                wait_piece(t, c, p_idx)
                for kc in range(NK):
                    ins = t.matmul(pp[:, half, :], ring_piece(p_idx)[:, kc, :],
                                   smid[:, kc, :],
                                   start=(kc == 0), stop=(kc == NK - 1))
                ins.then_inc(S.swf, 1)
            elif item[0] == "backT":
                i = item[1]
                t.wait_ge(S.sv_gt, C_SVGT * c + max(12 * i + 12, 37))
                if i >= 2:
                    t.wait_ge(S.sa_gt, C_SAGT * c + 4 + (i - 2) + 1)
                t.transpose(pEG[0:8, i % 2, 8:136], gpr[:, i, :],
                            ident3[:]).then_inc(S.st_gt, 1)
            elif item[0] == "gatebc":
                e = item[1]
                if e < 2:
                    t.wait_ge(S.sa_gt, C_SAGT * c + 8)
                else:
                    t.wait_ge(S.sa_gb, C_SAGB * c + e - 1)
                t.matmul(pEG[:, e % 2, :], selc8_sb[0:8, e, :], wdf[0:8, :],
                         start=True, stop=True).then_inc(S.st_gb, 1)
            else:
                ep, m, ii = item[1], item[2], item[3]
                q = 8 * ep + m
                if ii == 0:
                    if q == 0:
                        t.wait_ge(S.sa_gb, C_SAGB * c + 7)
                    elif q == 1:
                        t.wait_ge(S.sa_gb, C_SAGB * c + 8)
                    if m == 0:
                        t.wait_ge(S.sv_hg, C_SVHG * c + ep + 1)
                    if q >= 2:
                        t.wait_ge(S.sv_em, C_SVEM * c + q - 1)
                p_idx = 64 + 16 * ep + 2 * m + ii
                wait_piece(t, c, p_idx)
                for kc in range(NK):
                    ins = t.matmul(pEG[:, m % 2, :],
                                   ring_piece(p_idx)[:, kc, :],
                                   hg[:, ii, kc, :],
                                   start=(ii == 0 and kc == 0),
                                   stop=(ii == 1 and kc == NK - 1))
                ins.then_inc(S.swf, 1)

        for j in range(128):
            for item in T_INV.get(j, []):
                emit_item(item)
            gg, n = divmod(j, NST)
            s_ = j // 32
            if j >= 2:
                t.wait_ge(S.sv_bc, C_SVBC * c + j - 1)
            if n == 0:
                t.wait_ge(S.s_cs, 16 * (8 * c + gg + 1))
            if j % 32 == 0:
                t.wait_ge(S.sa_bp, C_SABP * c + s_ + 1)
            t.matmul(pBCb[:, j % 2, :], csel_sb[:, gg % 2, n, :],
                     b_sb[:, s_, :], start=True, stop=True).then_inc(S.st_bc, 1)
        for j2 in range(128, 140):
            for item in T_INV.get(j2, []):
                emit_item(item)


def kernel(**inputs):
    shared = pack_inputs(inputs)
    x = np.ascontiguousarray(np.asarray(inputs["L0_attn_x"], np.float32))
    nc = build()
    in_maps = []
    for cidx in range(NCORES):
        m = dict(shared)
        m["x"] = np.ascontiguousarray(x[2 * cidx:2 * cidx + 2])
        in_maps.append(m)
    res = run_bass_kernel_spmd(nc, in_maps, core_ids=list(range(NCORES)))
    return np.concatenate([r["out"] for r in res.results], axis=0)
